# version 02 (final)
# baseline (speedup 1.0000x reference)
import numpy as np
import ml_dtypes
from contextlib import ExitStack
import concourse.bass as bass
import concourse.mybir as mybir
from concourse.bass_utils import run_bass_kernel_spmd

F32 = mybir.dt.float32
BF16 = mybir.dt.bfloat16
AF = mybir.ActivationFunctionType
ALU = mybir.AluOpType
BF = ml_dtypes.bfloat16

D = 4096
T = 2048
INTER = 8192
NIN = 18560
NCH = 16
EPS = 1e-5


def build():
    nc = bass.Bass("TRN2", target_bir_lowering=False)
    dr = lambda nm, sh, dt=F32: nc.dram_tensor(nm, sh, dt, kind="ExternalInput").ap()
    x0T = dr("x0T", [D, T])
    flagd = dr("flag", [128, 1])
    W1T = dr("W1T", [D, D])
    B65d = dr("B65", [128, D])
    ATd = dr("ATm", [D, 64])
    gwTd = dr("gwT", [D, 4])
    gbd = dr("gbc", [4, 1])
    WinT = dr("WinT", [D, NIN])
    WoutT = dr("WoutT", [INTER, D])
    convd = dr("convd", [80, 4, 128, 128])
    convbd = dr("convb", [128, 80])
    dtbd = dr("dtb", [128, 1])
    Alogd = dr("Alog", [128, 1])
    Drowd = dr("Drow", [1, 128])
    NWcold = dr("NWcol", [128, 64])
    ID128d = nc.dram_tensor("ID128", [128, 128], BF16, kind="ExternalInput").ap()
    IDF32d = dr("IDF32", [128, 128])
    TRId = dr("TRI", [128, 128])
    onescd = nc.dram_tensor("onesc", [128, 1], BF16, kind="ExternalInput").ap()
    ef32d = dr("ef32", [1, 128])
    segd = dr("segsel", [1, 256])
    outD = nc.dram_tensor("outD", [D, 1024], F32, kind="ExternalOutput").ap()
    di = lambda nm, sh, dt=F32: nc.dram_tensor(nm, sh, dt, kind="Internal").ap()
    x1d = di("x1d", [D, T])
    x2d = di("x2d", [D, 1024])
    hbcd = di("hbcd", [81, 128, T + 3], BF16)
    cvxd = di("cvxd", [NCH, 64, 128, 128], BF16)
    cvbcd = di("cvbc", [NCH, 16, 128, 128], BF16)
    gatedd = di("gated", [64, 128, 1024], BF16)
    ygtd = di("ygtd", [8, 64, 128, 128], BF16)
    tmpd2 = di("tmpd2", [4, 1024])
    tmpd = di("tmpd", [128, 128])
    dbg1 = di("dbg1", [128, 32768], BF16)
    dbg2 = di("dbg2", [128, 1024], BF16)
    dbg3 = di("dbg3", [8, 1024])
    dbg4 = di("dbg4", [1024, 1024])
    dbg5 = di("dbg5", [1024, 1024])
    dbg6 = di("dbg6", [128, 1024])
    dbg7 = di("dbg7", [128, 1024])

    es = ExitStack()
    sbt = lambda nm, sh, dt: es.enter_context(nc.sbuf_tensor(nm, sh, dt))
    pst = lambda nm, sh, dt=F32: es.enter_context(nc.psum_tensor(nm, sh, dt))

    ABUF = sbt("ABUF", [128, 32768], BF16)
    WS = sbt("WS", [128, 8192], BF16)
    SRUN = sbt("SRUN", [128, INTER], F32)
    FB = sbt("FB", [128, 4096], F32)
    FB2 = sbt("FB2", [128, 1024], F32)
    DTF = sbt("DTF", [128, T], F32)
    CUMA = sbt("CUMA", [128, T], F32)
    UB = sbt("UB", [128, 6144], BF16)
    Z65 = sbt("Z65", [128, 1024], BF16)
    SM = sbt("SM", [128, 1536], F32)
    SMB = sbt("SMB", [128, 2304], BF16)
    CON = sbt("CON", [128, 264], F32)
    CONB = sbt("CONB", [128, 132], BF16)
    EFR = sbt("EFR", [1, 128], F32)
    GB = sbt("GB", [4, 1], F32)
    DRW = sbt("DRW", [1, 128], F32)
    NWC = sbt("NWC", [128, 64], F32)
    DT128 = sbt("DT128", [128, 128], F32)
    GWs = sbt("GWs", [128, 128], BF16)
    GSC = sbt("GSC", [8, 1024], F32)
    ROWS = sbt("ROWS", [1, 4096], F32)
    SEG1 = sbt("SEG1", [1, 256], F32)
    OROW = sbt("OROW", [1, 128], BF16)
    MTB = sbt("MTB", [128, 2048], BF16)
    CVB = sbt("CVB", [128, 80], F32)

    U_B65, U_AT = 0, 4096
    U_HB, U_CW, U_CV0 = 0, 2176, 2688
    U_SQ = 0
    XT, XD_, YBo, BCo, BTo = 0, 8192, 16384, 24576, 26624
    YGo = 28672

    PA = pst("PA", [128, 2048])
    PBm = pst("PB", [128, 1024])
    PT = pst("PT", [128, 1024], BF16)

    sems, val = {}, {}
    ops = {k: [] for k in ("tensor", "vector", "scalar", "sync", "gpsimd")}
    for s in ("ds", "da", "db", "dc", "dd", "wg", "wa", "wb", "tn", "vc", "sc", "dq"):
        sems[s] = es.enter_context(nc.semaphore(s))
        val[s] = 0

    def do(eng, fn, inc=None, wait=None):
        if eng == "vector" and val["vc"] > 0:
            ops[eng].append(lambda e, _v=val["vc"]: e.wait_ge(sems["vc"], _v))
        if eng == "scalar" and val["sc"] > 0:
            ops[eng].append(lambda e, _v=val["sc"]: e.wait_ge(sems["sc"], _v))
        if wait:
            for wn, wv in wait:
                if wv > 0:
                    ops[eng].append(lambda e, _s=sems[wn], _v=wv: e.wait_ge(_s, _v))
        if inc:
            nm, k = inc
            val[nm] += k
            ops[eng].append(lambda e, _fn=fn, _s=sems[nm], _k=k: _fn(e).then_inc(_s, _k))
            return val[nm]
        ops[eng].append(lambda e, _fn=fn: _fn(e))
        return None

    def barrier():
        v = dict(val)
        own = {"tensor": ("tn",), "vector": ("vc",), "scalar": ("sc",),
               "sync": (), "gpsimd": ()}
        for eng in ops:
            for s, x in v.items():
                if s not in own[eng] and s != "dq" and x > 0:
                    ops[eng].append(lambda e, _s=sems[s], _v=x: e.wait_ge(_s, _v))

    for o, i in [
        (CON[:, 0:128], IDF32d[:, :]), (CON[:, 128:256], TRId[:, :]),
        (CON[:, 256:257], flagd[:, :]), (CON[:, 257:258], dtbd[:, :]),
        (CON[:, 258:259], Alogd[:, :]),
        (CONB[:, 0:128], ID128d[:, :]), (CONB[:, 128:129], onescd[:, :]),
        (SEG1[:, :], segd[:, :]), (EFR[:, :], ef32d[:, :]),
        (GB[:, :], gbd[:, :]), (DRW[:, :], Drowd[:, :]), (NWC[:, :], NWcold[:, :]),
        (CVB[:, :], convbd[:, :]),
    ]:
        do("sync", lambda e, _o=o, _i=i: e.dma_start(out=_o, in_=_i), inc=("ds", 16))
    do("gpsimd", lambda e: e.dma_start(out=UB[:, U_B65:U_B65 + 4096], in_=B65d[:, :]), inc=("wg", 16))
    do("gpsimd", lambda e: e.dma_start(out=UB[:, U_AT:U_AT + 2048].rearrange("p (k r) -> p k r", k=32), in_=ATd.rearrange("(k p) r -> p k r", p=128)), inc=("wg", 16))
    do("gpsimd", lambda e: e.dma_start(out=GWs[:, :].rearrange("p (k r) -> p k r", k=32), in_=gwTd.rearrange("(k p) r -> p k r", p=128)), inc=("wg", 16))
    barrier()
    sA = do("scalar", lambda e: e.activation(CON[:, 259:260], CON[:, 258:259], AF.Exp), inc=("sc", 1))
    do("vector", lambda e: e.tensor_scalar_mul(CON[:, 259:260], CON[:, 259:260], -1.0), wait=[("sc", sA)], inc=("vc", 1))
    tD = do("tensor", lambda e: e.matmul(PBm[:, 0:128], EFR[:, :], DRW[:, :], start=True, stop=True), inc=("tn", 1))
    do("vector", lambda e: e.tensor_copy(DT128[:, :], PBm[:, 0:128]), wait=[("tn", tD)], inc=("vc", 1))
    z0 = do("vector", lambda e: e.memset(Z65[64:128, :], 0.0), inc=("vc", 1))
    do("vector", lambda e: e.memset(Z65[64:65, :], 1.0), wait=[("vc", z0)], inc=("vc", 1))
    do("vector", lambda e: e.memset(OROW[:, :], 1.0), inc=("vc", 1))
    do("vector", lambda e: e.memset(CON[:, 260:261], EPS), inc=("vc", 1))
    barrier()

    def ln_phase(src, th, tc):
        c0 = th * tc
        for o in range(32):
            slot = 2048 + (o % 2) * 1024
            sl = "da" if o % 2 == 0 else "db"
            dv = do("sync", lambda e, _o=o, _s=slot: e.dma_start(
                out=FB[:, _s:_s + tc], in_=src[_o * 128:(_o + 1) * 128, c0:c0 + tc]), inc=(sl, 16),
                wait=[("tn", val["tn"]), ("sc", val["sc"])] if o >= 2 else [])
            cv = do("vector", lambda e, _o=o, _s=slot: e.tensor_copy(ABUF[:, _o * 1024:_o * 1024 + tc], FB[:, _s:_s + tc]),
                    wait=[(sl, dv)], inc=("vc", 1))
            sv = do("scalar", lambda e, _o=o, _s=slot: e.activation(MTB[:, 0:tc], FB[:, _s:_s + tc], AF.Square),
                    wait=[(sl, dv), ("tn", val["tn"])], inc=("sc", 1))
            for t5 in range(tc // 512):
                do("tensor", lambda e, _o=o, _t=t5: e.matmul(
                    PA[0:1, _t * 512:(_t + 1) * 512], CONB[:, 128:129],
                    ABUF[:, _o * 1024 + _t * 512:_o * 1024 + (_t + 1) * 512],
                    start=(_o == 0), stop=(_o == 31)), wait=[("vc", cv)] if t5 == 0 else [], inc=("tn", 1))
                do("tensor", lambda e, _o=o, _t=t5: e.matmul(
                    PA[0:1, 1024 + _t * 512:1024 + (_t + 1) * 512], CONB[:, 128:129],
                    MTB[:, _t * 512:(_t + 1) * 512],
                    start=(_o == 0), stop=(_o == 31)), wait=[("sc", sv)] if t5 == 0 else [], inc=("tn", 1))
        barrier()
        do("vector", lambda e: e.tensor_copy(ROWS[0:1, 0:tc], PA[0:1, 0:tc]), inc=("vc", 1))
        do("vector", lambda e: e.tensor_copy(ROWS[0:1, 1024:1024 + tc], PA[0:1, 1024:1024 + tc]), inc=("vc", 1))
        do("vector", lambda e: e.tensor_scalar_mul(ROWS[0:1, 0:tc], ROWS[0:1, 0:tc], 1.0 / D), inc=("vc", 1))
        do("vector", lambda e: e.tensor_tensor(ROWS[0:1, 2048:2048 + tc], ROWS[0:1, 0:tc], ROWS[0:1, 0:tc], ALU.mult), inc=("vc", 1))
        vv = do("vector", lambda e: e.scalar_tensor_tensor(ROWS[0:1, 1024:1024 + tc], ROWS[0:1, 1024:1024 + tc], 1.0 / D, ROWS[0:1, 2048:2048 + tc], ALU.mult, ALU.subtract), inc=("vc", 1))
        s1 = do("scalar", lambda e: e.activation(ROWS[0:1, 1024:1024 + tc], ROWS[0:1, 1024:1024 + tc], AF.Ln, bias=CON[0:1, 260:261], scale=1.0), wait=[("vc", vv)], inc=("sc", 1))
        s2 = do("scalar", lambda e: e.activation(ROWS[0:1, 1024:1024 + tc], ROWS[0:1, 1024:1024 + tc], AF.Exp, bias=0.0, scale=-0.5), wait=[("sc", s1)], inc=("sc", 1))
        for j in range(tc // 512):
            do("tensor", lambda e, _j=j: e.matmul(PA[:, _j * 512:(_j + 1) * 512], EFR[:, :], ROWS[0:1, _j * 512:(_j + 1) * 512], start=True, stop=True),
               wait=[("sc", s2)] if j == 0 else [], inc=("tn", 1))
            do("tensor", lambda e, _j=j: e.matmul(PA[:, 1024 + _j * 512:1024 + (_j + 1) * 512], EFR[:, :], ROWS[0:1, 1024 + _j * 512:1024 + (_j + 1) * 512], start=True, stop=True), inc=("tn", 1))
        barrier()
        do("vector", lambda e: e.tensor_copy(FB[:, 0:tc], PA[:, 0:tc]), inc=("vc", 1))
        do("vector", lambda e: e.tensor_copy(FB[:, 1024:1024 + tc], PA[:, 1024:1024 + tc]), inc=("vc", 1))
        for o in range(32):
            slot = 2048 + (o % 2) * 1024
            sl = "da" if o % 2 == 0 else "db"
            dv = do("sync", lambda e, _o=o, _s=slot: e.dma_start(
                out=FB[:, _s:_s + tc], in_=src[_o * 128:(_o + 1) * 128, c0:c0 + tc]), inc=(sl, 16),
                wait=[("vc", val["vc"])])
            do("vector", lambda e, _o=o, _s=slot: e.tensor_tensor(FB2[:, 0:tc], FB[:, _s:_s + tc], FB[:, 0:tc], ALU.subtract),
               wait=[(sl, dv)], inc=("vc", 1))
            do("vector", lambda e, _o=o: e.tensor_tensor(ABUF[:, _o * 1024:_o * 1024 + tc], FB2[:, 0:tc], FB[:, 1024:1024 + tc], ALU.mult), inc=("vc", 1))
        barrier()

    def z_phase(tc):
        for k in range(32):
            for t5 in range(tc // 512):
                do("tensor", lambda e, _k=k, _t=t5: e.matmul(
                    PA[0:64, _t * 512:(_t + 1) * 512], UB[:, U_AT + _k * 64:U_AT + (_k + 1) * 64],
                    ABUF[:, _k * 1024 + _t * 512:_k * 1024 + (_t + 1) * 512],
                    start=(_k == 0), stop=(_k == 31)), inc=("tn", 1))
                do("tensor", lambda e, _k=k, _t=t5: e.matmul(
                    PBm[0:4, _t * 512:(_t + 1) * 512], GWs[:, _k * 4:(_k + 1) * 4],
                    ABUF[:, _k * 1024 + _t * 512:_k * 1024 + (_t + 1) * 512],
                    start=(_k == 0), stop=(_k == 31)), inc=("tn", 1))
        barrier()
        sv = do("scalar", lambda e: e.activation(GSC[0:4, 0:tc], PBm[0:4, 0:tc], AF.Identity, bias=GB[:, :], scale=1.0), inc=("sc", 1))
        d1 = do("sync", lambda e: e.dma_start(out=tmpd2[:, 0:tc], in_=GSC[0:4, 0:tc]), wait=[("sc", sv)], inc=("ds", 16))
        d2 = do("sync", lambda e: e.dma_start(out=FB[0:1, 0:4 * tc].rearrange("p (a b) -> p a b", a=4),
                                              in_=tmpd2[:, 0:tc].rearrange("a b -> () a b")),
                wait=[("ds", d1), ("vc", val["vc"])], inc=("ds", 16))
        S0 = lambda i: FB[0:1, i * tc:(i + 1) * tc]
        R0 = lambda i: ROWS[0:1, i * 1024:i * 1024 + tc]
        do("vector", lambda e: e.tensor_tensor(R0(0), S0(2), S0(3), ALU.max), wait=[("ds", d2)], inc=("vc", 1))
        do("vector", lambda e: e.tensor_tensor(R0(1), R0(0), S0(1), ALU.max), inc=("vc", 1))
        do("vector", lambda e: e.tensor_tensor(R0(1), R0(1), S0(0), ALU.is_gt), inc=("vc", 1))
        do("vector", lambda e: e.tensor_tensor(R0(2), S0(0), S0(1), ALU.max), inc=("vc", 1))
        do("vector", lambda e: e.tensor_tensor(R0(0), R0(0), R0(2), ALU.is_gt), inc=("vc", 1))
        do("vector", lambda e: e.tensor_tensor(R0(2), R0(2), S0(2), ALU.max), inc=("vc", 1))
        do("vector", lambda e: e.tensor_tensor(R0(2), S0(3), R0(2), ALU.is_gt), inc=("vc", 1))
        ov = do("vector", lambda e: e.memset(ROWS[0:1, 3072:3072 + tc], 1.0), inc=("vc", 1))
        mrows = [ROWS[0:1, 3072:4096], ROWS[0:1, 1024:2048], ROWS[0:1, 0:1024], ROWS[0:1, 2048:3072]]
        for j in range(tc // 512):
            for i in range(4):
                do("tensor", lambda e, _j=j, _i=i: e.matmul(
                    PBm[0:64, _j * 512:(_j + 1) * 512], SEG1[0:1, _i * 64:(_i + 1) * 64],
                    mrows[_i][:, _j * 512:(_j + 1) * 512],
                    start=(_i == 0), stop=(_i == 3)), wait=[("vc", ov)] if (j == 0 and i == 0) else [], inc=("tn", 1))
        barrier()
        mv = do("vector", lambda e: e.tensor_copy(SMB[0:64, 1024:1024 + tc], PBm[0:64, 0:tc]), inc=("vc", 1))
        do("vector", lambda e: e.tensor_tensor(Z65[0:64, 0:tc], PA[0:64, 0:tc], SMB[0:64, 1024:1024 + tc], ALU.mult),
           wait=[("vc", mv)], inc=("vc", 1))
        barrier()

    def ffn_main(th, tc, res_src, out_dram, final):
        c0 = th * tc
        for o in range(32):
            wsl = "wa" if o % 2 == 0 else "wb"
            sl = "da" if o % 2 == 0 else "db"
            wv = do("gpsimd", lambda e, _o=o: e.dma_start(
                out=WS[:, (_o % 2) * 4096:(_o % 2) * 4096 + 4096].rearrange("p (k f) -> p k f", k=32),
                in_=W1T[:, _o * 128:(_o + 1) * 128].rearrange("(k p) f -> p k f", p=128)),
                inc=(wsl, 16), wait=[("tn", val["tn"])] if o >= 2 else [])
            rv = do("sync", lambda e, _o=o: e.dma_start(
                out=FB[:, 2048 + (_o % 2) * 1024:2048 + (_o % 2) * 1024 + tc],
                in_=res_src[_o * 128:(_o + 1) * 128, c0:c0 + tc]), inc=(sl, 16),
                wait=[("vc", val["vc"])] if o >= 2 else [])
            for k in range(32):
                for t5 in range(tc // 512):
                    do("tensor", lambda e, _o=o, _k=k, _t=t5: e.matmul(
                        PA[:, _t * 512:(_t + 1) * 512],
                        WS[:, (_o % 2) * 4096 + _k * 128:(_o % 2) * 4096 + (_k + 1) * 128],
                        ABUF[:, _k * 1024 + _t * 512:_k * 1024 + (_t + 1) * 512],
                        start=(_k == 0), stop=False),
                       wait=([(wsl, wv), ("vc", val["vc"])] if (k == 0 and t5 == 0) else []), inc=("tn", 1))
            import os as _os2
            if _os2.environ.get("K_NOLORA", "0") != "1":
                for t5 in range(tc // 512):
                    do("tensor", lambda e, _o=o, _t=t5: e.matmul(
                        PA[:, _t * 512:(_t + 1) * 512], UB[:, U_B65 + _o * 128:U_B65 + (_o + 1) * 128],
                        Z65[:, _t * 512:(_t + 1) * 512], start=False, stop=True), inc=("tn", 1))
            else:
                for t5 in range(tc // 512):
                    do("tensor", lambda e, _o=o, _t=t5: e.matmul(
                        PA[:, _t * 512:(_t + 1) * 512], UB[:, U_B65 + _o * 128:U_B65 + (_o + 1) * 128],
                        Z65[:, _t * 512:(_t + 1) * 512], start=False, stop=True, skip_group_check=True), inc=("tn", 1)) if False else None
            tv = val["tn"]
            ev = do("vector", lambda e, _o=o: e.scalar_tensor_tensor(
                FB2[:, 0:tc], PA[:, 0:tc], 0.5, FB[:, 2048 + (_o % 2) * 1024:2048 + (_o % 2) * 1024 + tc],
                ALU.mult, ALU.add), wait=[("tn", tv), (sl, rv), ("dc", val["dc"])], inc=("vc", 1))
            do("sync", lambda e, _o=o: e.dma_start(out=out_dram[_o * 128:(_o + 1) * 128, c0:c0 + tc], in_=FB2[:, 0:tc]),
               wait=[("vc", ev)], inc=("dc", 16))
            import os as _os3
            if _os3.environ.get("K_DEBUG", "0") == "1" and o == 0 and (th == 0 if 'th' in dir() else True) and not final:
                do("sync", lambda e: e.dma_start(out=dbg6[:, 0:tc], in_=FB2[:, 0:tc]), wait=[("vc", ev)], inc=("dc", 16))
                pass
        barrier()

    import os as _os
    _DBG = _os.environ.get("K_DEBUG", "0") == "1"
    for th in range(2):
        ln_phase(x0T, th, 1024)
        if _DBG and th == 0:
            do("sync", lambda e: e.dma_start(out=dbg1[:, :], in_=ABUF[:, :]), inc=("ds", 16))
            barrier()
        z_phase(1024)
        if _DBG and th == 0:
            do("sync", lambda e: e.dma_start(out=dbg2[:, :], in_=Z65[:, :]), inc=("ds", 16))
            do("sync", lambda e: e.dma_start(out=dbg3[0:4, :], in_=GSC[0:4, :]), inc=("ds", 16))
            barrier()
        ffn_main(th, 1024, x0T, x1d, False)
        if _DBG and th == 0:
            do("sync", lambda e: e.dma_start(out=dbg5[:, :], in_=x1d[0:1024, 0:1024]), inc=("ds", 16))
            barrier()
        ln_phase(x1d, th, 1024)
        for ct in range(145):
            wsl = "wa" if ct % 2 == 0 else "wb"
            wv = do("gpsimd", lambda e, _c=ct: e.dma_start(
                out=WS[:, (_c % 2) * 4096:(_c % 2) * 4096 + 4096].rearrange("p (k f) -> p k f", k=32),
                in_=WinT[:, _c * 128:(_c + 1) * 128].rearrange("(k p) f -> p k f", p=128)),
                inc=(wsl, 16), wait=[("tn", val["tn"])] if ct >= 2 else [])
            for k in range(32):
                for t5 in range(2):
                    do("tensor", lambda e, _c=ct, _k=k, _t=t5: e.matmul(
                        PA[:, _t * 512:(_t + 1) * 512],
                        WS[:, (_c % 2) * 4096 + _k * 128:(_c % 2) * 4096 + (_k + 1) * 128],
                        ABUF[:, _k * 1024 + _t * 512:_k * 1024 + (_t + 1) * 512],
                        start=(_k == 0), stop=(_k == 31)),
                       wait=([(wsl, wv), ("vc", val["vc"]), ("ds", val["ds"])] if (k == 0 and t5 == 0) else []),
                       inc=("tn", 1))
            tv = val["tn"]
            if ct < 64:
                if th == 1:
                    ev = do("vector", lambda e: e.tensor_copy(SMB[:, 0:1024], PA[:, 0:1024]), wait=[("tn", tv)], inc=("vc", 1))
                    do("sync", lambda e, _c=ct: e.dma_start(out=gatedd[_c][:, 0:1024], in_=SMB[:, 0:1024]),
                       wait=[("vc", ev)], inc=("ds", 16))
                else:
                    do("vector", lambda e: e.tensor_copy(SMB[:, 0:1024], PA[:, 0:1024]), wait=[("tn", tv)], inc=("vc", 1))
            elif ct < 144:
                ev = do("vector", lambda e: e.tensor_copy(SMB[:, 0:1024], PA[:, 0:1024]), wait=[("tn", tv)], inc=("vc", 1))
                do("sync", lambda e, _c=ct, _th=th: e.dma_start(
                    out=hbcd[_c - 64][:, 3 + _th * 1024:3 + _th * 1024 + 1024], in_=SMB[:, 0:1024]),
                   wait=[("vc", ev)], inc=("ds", 16))
            else:
                do("vector", lambda e, _th=th: e.tensor_copy(DTF[:, _th * 1024:_th * 1024 + 1024], PA[:, 0:1024]),
                   wait=[("tn", tv)], inc=("vc", 1))
        barrier()
    if _DBG:
        do("sync", lambda e: e.dma_start(out=dbg4[:, :], in_=x1d[0:1024, 0:1024]), inc=("ds", 16))
        barrier()
    zv = do("vector", lambda e: e.memset(SMB[:, 0:4], 0.0), inc=("vc", 1))
    for ctt in range(81):
        do("sync", lambda e, _c=ctt: e.dma_start(out=hbcd[_c][:, 0:3], in_=SMB[:, 0:3]), wait=[("vc", zv)], inc=("ds", 16))
    barrier()

    for ct in range(80):
        wv = do("gpsimd", lambda e, _c=ct: e.dma_start(
            out=UB[:, U_CW:U_CW + 512].rearrange("p (k f) -> p k f", k=4), in_=convd[_c].rearrange("k p f -> p k f")),
            inc=("wg", 16), wait=[("tn", val["tn"])] if ct >= 1 else [])
        hv = do("sync", lambda e, _c=ct: e.dma_start(out=UB[:, U_HB:U_HB + T + 3], in_=hbcd[_c][:, :]),
                inc=("ds", 16), wait=[("tn", val["tn"])] if ct >= 1 else [])
        fv = do("vector", lambda e: e.tensor_scalar_mul(UB[:, U_HB + 1024:U_HB + 1027], UB[:, U_HB + 1024:U_HB + 1027], CON[:, 256:257]),
                wait=[("ds", hv), ("sc", val["sc"])], inc=("vc", 1))
        for k in range(4):
            for t5 in range(4):
                do("tensor", lambda e, _k=k, _t=t5: e.matmul(
                    PA[:, _t * 512:(_t + 1) * 512], UB[:, U_CW + _k * 128:U_CW + (_k + 1) * 128],
                    UB[:, U_HB + _k + _t * 512:U_HB + _k + _t * 512 + 512],
                    start=(_k == 0), stop=(_k == 3)),
                   wait=([("wg", wv), ("vc", fv)] if (k == 0 and t5 == 0) else []), inc=("tn", 1))
        tv = val["tn"]
        sv = do("scalar", lambda e, _c=ct: e.activation(UB[:, U_CV0:U_CV0 + 2048], PA[:, 0:2048], AF.Silu, bias=CVB[:, _c:_c + 1], scale=1.0),
                wait=[("tn", tv)], inc=("sc", 1))
        if ct < 64:
            do("sync", lambda e, _c=ct: e.dma_start(out=cvxd[:, _c, :, :].rearrange("c p f -> p c f"), in_=UB[:, U_CV0:U_CV0 + 2048].rearrange("p (c f) -> p c f", c=16)),
               wait=[("sc", sv)], inc=("ds", 16))
        else:
            do("sync", lambda e, _c=ct: e.dma_start(out=cvbcd[:, _c - 64, :, :].rearrange("c p f -> p c f"), in_=UB[:, U_CV0:U_CV0 + 2048].rearrange("p (c f) -> p c f", c=16)),
               wait=[("sc", sv)], inc=("ds", 16))
    barrier()

    s1 = do("scalar", lambda e: e.activation(FB[:, 0:2048], DTF[:, :], AF.Exp, bias=CON[:, 257:258], scale=1.0), inc=("sc", 1))
    v1 = do("vector", lambda e: e.tensor_scalar_add(FB[:, 0:2048], FB[:, 0:2048], 1.0), wait=[("sc", s1)], inc=("vc", 1))
    s2 = do("scalar", lambda e: e.activation(DTF[:, :], FB[:, 0:2048], AF.Ln), wait=[("vc", v1)], inc=("sc", 1))
    v2 = do("vector", lambda e: e.tensor_scalar_mul(FB[:, 0:2048], DTF[:, :], CON[:, 259:260]), wait=[("sc", s2)], inc=("vc", 1))
    vo = do("vector", lambda e: e.memset(SM[:, 1152:1280], 1.0), inc=("vc", 1))
    for ch in range(NCH):
        do("vector", lambda e, _c=ch: e.tensor_tensor_scan(
            CUMA[:, _c * 128:(_c + 1) * 128], SM[:, 1152:1280], FB[:, _c * 128:(_c + 1) * 128],
            0.0, ALU.mult, ALU.add), wait=[("vc", vo)] if ch == 0 else [], inc=("vc", 1))
    do("vector", lambda e: e.memset(SRUN[:, :], 0.0), inc=("vc", 1))
    barrier()

    S_ = lambda i: SM[:, i * 128:(i + 1) * 128]
    for c in range(NCH):
        own = c >= 8
        cc = c * 128
        dv1 = do("scalar", lambda e, _c=cc: e.activation(S_(0), CUMA[:, _c:_c + 128], AF.Exp,
                                                         bias=CUMA[:, _c + 127:_c + 128], scale=-1.0), inc=("sc", 1))
        dv2 = do("vector", lambda e, _c=cc: e.tensor_tensor(S_(1), S_(0), DTF[:, _c:_c + 128], ALU.mult), wait=[("sc", dv1)], inc=("vc", 1))
        do("tensor", lambda e: e.transpose(PA[:, 0:128], S_(1), CON[:, 0:128]), wait=[("vc", dv2)], inc=("tn", 1))
        do("tensor", lambda e, _c=cc: e.transpose(PA[:, 512:640], CUMA[:, _c:_c + 128], CON[:, 0:128]), inc=("tn", 1))
        t3 = do("tensor", lambda e, _c=cc: e.transpose(PA[:, 1024:1152], DTF[:, _c:_c + 128], CON[:, 0:128]), inc=("tn", 1))
        do("vector", lambda e: e.tensor_copy(S_(2), PA[:, 0:128]), wait=[("tn", t3)], inc=("vc", 1))
        do("vector", lambda e: e.tensor_copy(S_(3), PA[:, 512:640]), inc=("vc", 1))
        do("vector", lambda e: e.tensor_copy(S_(5), PA[:, 1024:1152]), inc=("vc", 1))
        do("vector", lambda e: e.tensor_scalar_mul(S_(4), S_(3), -1.0), inc=("vc", 1))
        barrier()
        xv = do("sync", lambda e, _c=c: e.dma_start(out=ABUF[:, XT:XT + 8192].rearrange("p (t f) -> p t f", t=64), in_=cvxd[_c].rearrange("t p f -> p t f")),
                inc=("ds", 16))
        bv = do("sync", lambda e, _c=c: e.dma_start(out=ABUF[:, BCo:BCo + 2048].rearrange("p (t f) -> p t f", t=16), in_=cvbcd[_c].rearrange("t p f -> p t f")),
                inc=("ds", 16))
        for i in range(64):
            do("tensor", lambda e, _i=i: e.transpose(PT[:, (_i % 8) * 128:(_i % 8) * 128 + 128],
                                                     ABUF[:, XT + _i * 128:XT + (_i + 1) * 128], CONB[:, 0:128]),
               wait=([("ds", bv)] if i == 0 else []) + ([("vc", val["vc"])] if (i % 8 == 0 and i > 0) else []), inc=("tn", 1))
            if i % 8 == 7:
                tv = val["tn"]
                do("vector", lambda e, _i=i: e.tensor_copy(ABUF[:, XD_ + (_i - 7) * 128:XD_ + (_i + 1) * 128], PT[:, 0:1024]),
                   wait=[("tn", tv)], inc=("vc", 1))
        for g in range(8):
            do("tensor", lambda e, _g=g: e.transpose(PT[:, _g * 128:(_g + 1) * 128],
                                                     ABUF[:, BCo + _g * 128:BCo + (_g + 1) * 128], CONB[:, 0:128]),
               wait=([("ds", bv)] if g == 0 else []) + ([("vc", val["vc"])] if g == 0 else []), inc=("tn", 1))
        tv = val["tn"]
        do("vector", lambda e: e.tensor_copy(ABUF[:, BTo:BTo + 1024], PT[:, 0:1024]), wait=[("tn", tv)], inc=("vc", 1))
        if c == 8:
            do("vector", lambda e: e.tensor_scalar_mul(SRUN[:, :], SRUN[:, :], CON[:, 256:257]), inc=("vc", 1))
        barrier()

        if own:
            cp = c - 8
            do("vector", lambda e: e.tensor_copy(ABUF[:, XT:XT + 8192], SRUN[:, :]), inc=("vc", 1))
            for g in range(8):
                tg = do("tensor", lambda e, _g=g: e.matmul(PBm[:, _g * 128:(_g + 1) * 128],
                                                           ABUF[:, BCo + _g * 128:BCo + (_g + 1) * 128],
                                                           ABUF[:, BCo + 1024 + _g * 128:BCo + 1024 + (_g + 1) * 128],
                                                           start=True, stop=True), inc=("tn", 1))
                do("vector", lambda e, _g=g: e.tensor_tensor(SMB[:, _g * 128:(_g + 1) * 128],
                                                             PBm[:, _g * 128:(_g + 1) * 128], CON[:, 128:256], ALU.mult),
                   wait=[("tn", tg)], inc=("vc", 1))
            sx = do("scalar", lambda e, _c=cc: e.activation(SM[:, 9 * 128:10 * 128], CUMA[:, _c:_c + 128], AF.Exp), inc=("sc", 1))
            tx = do("tensor", lambda e: e.transpose(PA[:, 0:128], SM[:, 9 * 128:10 * 128], CON[:, 0:128]), wait=[("sc", sx)], inc=("tn", 1))
            do("vector", lambda e: e.tensor_copy(S_(6), PA[:, 0:128]), wait=[("tn", tx)], inc=("vc", 1))
            dcu = do("sync", lambda e, _c=cc: e.dma_start(out=tmpd[:, :], in_=CUMA[:, _c:_c + 128]), inc=("ds", 16))
            barrier()
            for sb in range(8):
                if sb % 2 == 0:
                    do("sync", lambda e, _q=sb // 2: e.dma_start(
                        out=ROWS[0:1, :].rearrange("p (a b) -> p a b", a=32),
                        in_=tmpd[_q * 32:(_q + 1) * 32, :].rearrange("a b -> () a b")),
                       inc=("ds", 16), wait=[("tn", val["tn"]), ("ds", dcu)])
                for hh in range(16):
                    h = sb * 16 + hh
                    g = h // 16
                    tr = do("tensor", lambda e, _h=h: e.matmul(
                        PBm[:, (_h % 2) * 512:(_h % 2) * 512 + 128],
                        EFR[:, :], ROWS[0:1, (_h % 32) * 128:(_h % 32) * 128 + 128], start=True, stop=True),
                        wait=([("vc", val["vc"])] if hh > 1 else []) + ([("ds", val["ds"])] if hh == 0 else []), inc=("tn", 1))
                    lt = do("vector", lambda e, _h=h: e.tensor_scalar(
                        FB2[:, (_h % 2) * 128:(_h % 2) * 128 + 128],
                        PBm[:, (_h % 2) * 512:(_h % 2) * 512 + 128],
                        SM[:, 4 * 128 + _h:4 * 128 + _h + 1],
                        87.0, ALU.add, ALU.min), wait=[("tn", tr), ("sc", val["sc"])], inc=("vc", 1))
                    le = do("scalar", lambda e, _h=h: e.activation(
                        SMB[:, 1152 + (_h % 2) * 128:1152 + (_h % 2) * 128 + 128],
                        FB2[:, (_h % 2) * 128:(_h % 2) * 128 + 128], AF.Exp),
                        wait=[("vc", lt)], inc=("sc", 1))
                    mt = do("vector", lambda e, _h=h, _g=g: e.scalar_tensor_tensor(
                        MTB[:, (_h % 8) * 256:(_h % 8) * 256 + 128],
                        SMB[:, 1152 + (_h % 2) * 128:1152 + (_h % 2) * 128 + 128],
                        SM[:, 5 * 128 + _h:5 * 128 + _h + 1],
                        SMB[:, _g * 128:(_g + 1) * 128], ALU.mult, ALU.mult),
                        wait=[("sc", le), ("tn", val["tn"])], inc=("vc", 1))
                    xd = do("vector", lambda e, _h=h: e.tensor_scalar_mul(
                        MTB[:, (_h % 8) * 256 + 128:(_h % 8) * 256 + 192],
                        ABUF[:, XD_ + _h * 64:XD_ + (_h + 1) * 64], DT128[:, _h:_h + 1]),
                        wait=[("vc", mt)], inc=("vc", 1))
                    do("tensor", lambda e, _h=h, _hh=hh: e.matmul(
                        PA[:, _hh * 64:(_hh + 1) * 64], MTB[:, (_h % 8) * 256:(_h % 8) * 256 + 128],
                        ABUF[:, XD_ + _h * 64:XD_ + (_h + 1) * 64], start=True, stop=False),
                       wait=[("vc", xd)], inc=("tn", 1))
                    do("tensor", lambda e, _h=h: e.matmul(
                        PA[:, (_h % 16) * 64:(_h % 16) * 64 + 64], CONB[:, 0:128],
                        MTB[:, (_h % 8) * 256 + 128:(_h % 8) * 256 + 192], start=False, stop=True), inc=("tn", 1))
                    do("tensor", lambda e, _h=h, _hh=hh, _g=g: e.matmul(
                        PA[:, 1024 + _hh * 64:1024 + (_hh + 1) * 64],
                        ABUF[:, BCo + 1024 + _g * 128:BCo + 1024 + (_g + 1) * 128],
                        ABUF[:, XT + _h * 64:XT + (_h + 1) * 64], start=True, stop=True), inc=("tn", 1))
                tv = val["tn"]
                for hh in range(16):
                    h = sb * 16 + hh
                    do("vector", lambda e, _h=h, _hh=hh: e.tensor_scalar_mul(
                        FB2[:, _hh * 64:(_hh + 1) * 64], PA[:, 1024 + _hh * 64:1024 + (_hh + 1) * 64],
                        SM[:, 6 * 128 + _h:6 * 128 + _h + 1]),
                       wait=[("tn", tv)] if hh == 0 else [], inc=("vc", 1))
                do("vector", lambda e, _sb=sb: e.tensor_tensor(ABUF[:, YBo + _sb * 1024:YBo + (_sb + 1) * 1024],
                                                               FB2[:, 0:1024], PA[:, 0:1024], ALU.add),
                   inc=("vc", 1))
            barrier()
            for i in range(64):
                do("tensor", lambda e, _i=i: e.transpose(PT[:, (_i % 8) * 128:(_i % 8) * 128 + 128],
                                                         ABUF[:, YBo + _i * 128:YBo + (_i + 1) * 128], CONB[:, 0:128]),
                   wait=[("vc", val["vc"])] if (i % 8 == 0 and i > 0) else [], inc=("tn", 1))
                if i % 8 == 7:
                    tv = val["tn"]
                    do("vector", lambda e, _i=i: e.tensor_copy(ABUF[:, XT + (_i - 7) * 128:XT + (_i + 1) * 128], PT[:, 0:1024]),
                       wait=[("tn", tv)], inc=("vc", 1))
            barrier()
            for i in range(64):
                sqv = do("scalar", lambda e, _i=i: e.activation(
                    SMB[:, 1280 + (_i % 2) * 128:1280 + (_i % 2) * 128 + 128],
                    ABUF[:, XT + _i * 128:XT + (_i + 1) * 128], AF.Square),
                    wait=[("tn", val["tn"])] if i >= 2 else [], inc=("sc", 1))
                do("tensor", lambda e, _i=i: e.matmul(PBm[0:1, 0:128], CONB[:, 128:129],
                                                      SMB[:, 1280 + (_i % 2) * 128:1280 + (_i % 2) * 128 + 128],
                                                      start=(_i == 0), stop=(_i == 63)),
                   wait=[("sc", sqv)], inc=("tn", 1))
            barrier()
            do("vector", lambda e: e.tensor_copy(ROWS[0:1, 0:128], PBm[0:1, 0:128]), inc=("vc", 1))
            vv = do("vector", lambda e: e.tensor_scalar_mul(ROWS[0:1, 0:128], ROWS[0:1, 0:128], 1.0 / INTER), inc=("vc", 1))
            sr1 = do("scalar", lambda e: e.activation(ROWS[0:1, 0:128], ROWS[0:1, 0:128], AF.Ln, bias=CON[0:1, 260:261], scale=1.0),
                     wait=[("vc", vv)], inc=("sc", 1))
            sr2 = do("scalar", lambda e: e.activation(ROWS[0:1, 0:128], ROWS[0:1, 0:128], AF.Exp, bias=0.0, scale=-0.5),
                     wait=[("sc", sr1)], inc=("sc", 1))
            tb = do("tensor", lambda e: e.matmul(PBm[:, 128:256], EFR[:, :], ROWS[0:1, 0:128], start=True, stop=True),
                    wait=[("sc", sr2)], inc=("tn", 1))
            do("vector", lambda e: e.tensor_copy(S_(7), PBm[:, 128:256]), wait=[("tn", tb)], inc=("vc", 1))
            barrier()
            for i in range(64):
                gsl = "da" if i % 2 == 0 else "db"
                gv = do("sync", lambda e, _i=i, _cp=cp: e.dma_start(
                    out=SMB[:, 1024 + (_i % 2) * 128:1024 + (_i % 2) * 128 + 128],
                    in_=gatedd[_i][:, _cp * 128:(_cp + 1) * 128]), inc=(gsl, 16),
                    wait=[("vc", val["vc"])] if i >= 2 else [])
                sg = do("scalar", lambda e, _i=i: e.activation(FB2[:, (_i % 2) * 128:(_i % 2) * 128 + 128], SMB[:, 1024 + (_i % 2) * 128:1024 + (_i % 2) * 128 + 128], AF.Silu),
                        wait=[(gsl, gv)], inc=("sc", 1))
                t1 = do("vector", lambda e, _i=i: e.scalar_tensor_tensor(
                    SMB[:, 1280:1408], ABUF[:, XT + _i * 128:XT + (_i + 1) * 128], NWC[:, _i:_i + 1],
                    S_(7), ALU.mult, ALU.mult), wait=[("sc", sg)], inc=("vc", 1))
                do("vector", lambda e, _i=i: e.tensor_tensor(ABUF[:, XD_ + _i * 128:XD_ + (_i + 1) * 128],
                                                             SMB[:, 1280:1408], FB2[:, (_i % 2) * 128:(_i % 2) * 128 + 128], ALU.mult),
                   wait=[("vc", t1)], inc=("vc", 1))
            barrier()
            do("sync", lambda e, _cp=cp: e.dma_start(out=ygtd[_cp].rearrange("t p f -> p t f"),
                                                     in_=ABUF[:, XD_:XD_ + 8192].rearrange("p (t f) -> p t f", t=64)), inc=("ds", 16))
            barrier()
            xv2 = do("sync", lambda e, _c=c: e.dma_start(out=ABUF[:, XT:XT + 8192].rearrange("p (t f) -> p t f", t=64), in_=cvxd[_c].rearrange("t p f -> p t f")),
                     inc=("ds", 16))
            for i in range(64):
                do("tensor", lambda e, _i=i: e.transpose(PT[:, (_i % 8) * 128:(_i % 8) * 128 + 128],
                                                         ABUF[:, XT + _i * 128:XT + (_i + 1) * 128], CONB[:, 0:128]),
                   wait=([("ds", xv2)] if i == 0 else []) + ([("vc", val["vc"])] if (i % 8 == 0 and i > 0) else []), inc=("tn", 1))
                if i % 8 == 7:
                    tv = val["tn"]
                    do("vector", lambda e, _i=i: e.tensor_copy(ABUF[:, XD_ + (_i - 7) * 128:XD_ + (_i + 1) * 128], PT[:, 0:1024]),
                       wait=[("tn", tv)], inc=("vc", 1))
            barrier()

        for h in range(128):
            do("vector", lambda e, _h=h: e.tensor_scalar_mul(
                ABUF[:, YBo + _h * 64:YBo + (_h + 1) * 64], ABUF[:, XD_ + _h * 64:XD_ + (_h + 1) * 64],
                SM[:, 2 * 128 + _h:2 * 128 + _h + 1]), inc=("vc", 1))
        se = do("scalar", lambda e, _c=cc: e.activation(SM[:, 1408:1409], CUMA[:, _c + 127:_c + 128], AF.Exp),
                inc=("sc", 1))
        te = do("tensor", lambda e: e.transpose(PA[0:1, 384:512], SM[:, 1408:1409], CON[:, 0:128]), wait=[("sc", se)], inc=("tn", 1))
        ee = do("vector", lambda e: e.tensor_copy(SM[0:1, 1280:1408], PA[0:1, 384:512]), wait=[("tn", te)], inc=("vc", 1))
        tc4 = do("tensor", lambda e: e.matmul(PA[:, 256:384], EFR[:, :], SM[0:1, 1280:1408], start=True, stop=True),
                 wait=[("vc", ee)], inc=("tn", 1))
        ecv2 = do("vector", lambda e: e.tensor_copy(FB2[:, 128:256], PA[:, 256:384]), wait=[("tn", tc4)], inc=("vc", 1))
        barrier()
        for g in range(8):
            for j in range(2):
                do("tensor", lambda e, _g=g, _j=j: e.matmul(
                    PBm[:, _j * 512:(_j + 1) * 512], ABUF[:, BTo + _g * 128:BTo + (_g + 1) * 128],
                    ABUF[:, YBo + _g * 1024 + _j * 512:YBo + _g * 1024 + (_j + 1) * 512],
                    start=True, stop=True), wait=[("vc", val["vc"])] if j == 0 else [], inc=("tn", 1))
            tvg = val["tn"]
            for hh in range(16):
                h = g * 16 + hh
                do("vector", lambda e, _h=h: e.tensor_scalar_mul(
                    FB2[:, 0:64], SRUN[:, _h * 64:(_h + 1) * 64], FB2[:, 128 + _h:128 + _h + 1]),
                   wait=[("tn", tvg)] if hh == 0 else [], inc=("vc", 1))
                do("vector", lambda e, _h=h, _hh=hh: e.tensor_tensor(
                    SRUN[:, _h * 64:(_h + 1) * 64], FB2[:, 0:64],
                    PBm[:, _hh * 64:(_hh + 1) * 64], ALU.add), inc=("vc", 1))
        barrier()

    for oh in range(2):
        for cpp in range(8):
            do("sync", lambda e, _cp=cpp, _oh=oh: e.dma_start(
                out=ABUF.rearrange("p (k t) -> p k t", k=64)[:, :, _cp * 64:(_cp + 1) * 64] if False else
                    ABUF[:, 0:32768].rearrange("p (k t) -> p k t", k=64)[:, :, _cp * 64:(_cp + 1) * 64],
                in_=ygtd[_cp].rearrange("t p f -> p t f")[:, :, _oh * 64:(_oh + 1) * 64]), inc=("ds", 16),
               wait=[("tn", val["tn"])] if (oh > 0 and cpp == 0) else [])
        barrier()
        for o in range(32):
            wv = do("gpsimd", lambda e, _o=o: e.dma_start(
                out=WS[:, 0:8192].rearrange("p (k f) -> p k f", k=64),
                in_=WoutT[:, _o * 128:(_o + 1) * 128].rearrange("(k p) f -> p k f", p=128)),
                inc=("wg", 16), wait=[("tn", val["tn"])])
            sl = "da" if o % 2 == 0 else "db"
            rv = do("sync", lambda e, _o=o, _oh=oh: e.dma_start(
                out=FB[:, 2048 + (_o % 2) * 1024:2048 + (_o % 2) * 1024 + 512].rearrange(
                    "p (cp j) -> p cp j", cp=8),
                in_=x1d[_o * 128:(_o + 1) * 128, 1024:2048].rearrange(
                    "p (cp two j) -> p cp two j", cp=8, two=2)[:, :, _oh, :]), inc=(sl, 16),
                wait=[("vc", val["vc"])] if o >= 2 else [])
            for k in range(64):
                do("tensor", lambda e, _o=o, _k=k: e.matmul(
                    PA[:, 0:512], WS[:, _k * 128:(_k + 1) * 128],
                    ABUF[:, _k * 512:(_k + 1) * 512], start=(_k == 0), stop=(_k == 63)),
                   wait=[("wg", wv), ("vc", val["vc"])] if k == 0 else [], inc=("tn", 1))
            tv = val["tn"]
            ev = do("vector", lambda e, _o=o: e.tensor_tensor(
                FB2[:, 0:512], PA[:, 0:512], FB[:, 2048 + (_o % 2) * 1024:2048 + (_o % 2) * 1024 + 512], ALU.add),
                wait=[("tn", tv), (sl, rv), ("dc", val["dc"])], inc=("vc", 1))
            do("sync", lambda e, _o=o, _oh=oh: e.dma_start(
                out=x2d[_o * 128:(_o + 1) * 128, _oh * 512:(_oh + 1) * 512], in_=FB2[:, 0:512]),
               wait=[("vc", ev)], inc=("dc", 16))
            barrier()
    barrier()

    do("gpsimd", lambda e: e.dma_start(out=UB[:, U_B65:U_B65 + 4096], in_=B65d[:, :]), inc=("wg", 16))
    do("gpsimd", lambda e: e.dma_start(out=UB[:, U_AT:U_AT + 2048].rearrange("p (k r) -> p k r", k=32), in_=ATd.rearrange("(k p) r -> p k r", p=128)), inc=("wg", 16))
    barrier()
    for oh in range(2):
        ln_phase(x2d, oh, 512)
        z_phase(512)
        ffn_main(oh, 512, x2d, outD, True)

    do("sync", lambda e: e.nop(), wait=[("dc", val["dc"]), ("dd", val["dd"])])

    with nc.Block() as block:
        @block.tensor
        def _(t):
            for f in ops["tensor"]:
                f(t)

        @block.vector
        def _(v):
            for f in ops["vector"]:
                f(v)

        @block.scalar
        def _(s):
            for f in ops["scalar"]:
                f(s)

        @block.sync
        def _(sy):
            for f in ops["sync"]:
                f(sy)

        @block.gpsimd
        def _(g):
            for f in ops["gpsimd"]:
                f(g)

    es.close()
    return nc


_NC = None


def kernel(**inputs):
    global _NC
    f32 = np.float32
    x = np.asarray(inputs["L0_attn_hidden_states"], f32)
    inw = np.asarray(inputs["L0_attn_in_proj_w"], f32)
    convw = np.asarray(inputs["L0_attn_conv_w"], f32)
    convb = np.asarray(inputs["L0_attn_conv_b"], f32)
    dtb = np.asarray(inputs["L0_attn_dt_bias"], f32)
    alog = np.asarray(inputs["L0_attn_A_log"], f32)
    Dv = np.asarray(inputs["L0_attn_D"], f32)
    nw = np.asarray(inputs["L0_attn_norm_w"], f32)
    outw = np.asarray(inputs["L0_attn_out_proj_w"], f32)
    W = np.asarray(inputs["L0_ffn_W"], f32)
    b = np.asarray(inputs["L0_ffn_b"], f32)
    A = np.asarray(inputs["L0_ffn_A"], f32)
    Bm = np.asarray(inputs["L0_ffn_B"], f32)
    gw = np.asarray(inputs["L0_ffn_gate_w"], f32)
    gb = np.asarray(inputs["L0_ffn_gate_b"], f32)

    W1T = np.ascontiguousarray(W.T)
    B65 = np.zeros((128, D), f32)
    B65[0:64] = Bm.T
    B65[64] = b
    ATm = np.ascontiguousarray(A.T)
    gwT = np.ascontiguousarray(gw.T)
    WinTh = np.ascontiguousarray(inw.T)
    WoutTh = np.ascontiguousarray(outw.T)
    convd = np.zeros((80, 4, 128, 128), f32)
    cw = convw[:, 0, :]
    for t in range(80):
        for k in range(4):
            np.fill_diagonal(convd[t, k], cw[t * 128:(t + 1) * 128, k])
    convbh = np.ascontiguousarray(convb.reshape(80, 128).T)
    seg = np.zeros((1, 256), f32)
    seg[0, 0:8] = 1
    seg[0, 64 + 8:64 + 16] = 1
    seg[0, 128 + 16:128 + 32] = 1
    seg[0, 192 + 32:192 + 64] = 1
    common = dict(
        W1T=W1T, B65=B65, ATm=ATm, gwT=gwT, gbc=gb.reshape(4, 1),
        WinT=WinTh, WoutT=WoutTh, convd=convd, convb=convbh,
        dtb=dtb.reshape(128, 1), Alog=alog.reshape(128, 1),
        Drow=Dv.reshape(1, 128), NWcol=np.ascontiguousarray(nw.reshape(64, 128).T),
        ID128=np.eye(128, dtype=BF), IDF32=np.eye(128, dtype=f32),
        TRI=np.ascontiguousarray(np.triu(np.ones((128, 128), f32))),
        onesc=np.ones((128, 1), BF), ef32=np.ones((1, 128), f32),
        segsel=seg,
    )
    in_maps = []
    for core in range(8):
        bidx, half = core // 2, core % 2
        own = x[bidx, half * 1024:(half + 1) * 1024, :]
        prev = x[bidx, 0:1024, :] if half == 1 else own
        loc = np.concatenate([prev, own], axis=0)
        m = dict(common)
        m["x0T"] = np.ascontiguousarray(loc.T)
        m["flag"] = np.full((128, 1), 1.0 if half == 1 else 0.0, f32)
        in_maps.append(m)

    if _NC is None:
        _NC = build()
    res = run_bass_kernel_spmd(_NC, in_maps, core_ids=list(range(8)))
    out = np.zeros((4, 2048, 4096), np.float32)
    perm = np.zeros(1024, np.int64)
    for oh in range(2):
        for cp in range(8):
            for j in range(64):
                perm[oh * 512 + cp * 64 + j] = cp * 128 + oh * 64 + j
    for core in range(8):
        bidx, half = core // 2, core % 2
        o = res.results[core]["outD"]
        unperm = np.zeros_like(o)
        unperm[:, perm] = o
        out[bidx, half * 1024:(half + 1) * 1024, :] = unperm.T
    return out
